# version 27
# baseline (speedup 1.0000x reference)
import contextlib
import numpy as np
import ml_dtypes

import concourse.bass as bass
import concourse.mybir as mybir
from concourse.bass_utils import run_bass_kernel_spmd

F32 = mybir.dt.float32
BF = mybir.dt.bfloat16
AF = mybir.ActivationFunctionType
OP = mybir.AluOpType
AX = mybir.AxisListType
BFNP = ml_dtypes.bfloat16

D = 1024
NCH = 8
NT = 16
TT = NT * 128
W = TT + 128
TAPS = 40
EPS = 1e-5
BETA = float(D) ** -0.5


class Sems:
    def __init__(self, nc, es, names):
        self.h = {n: es.enter_context(nc.semaphore(n)) for n in names}
        self.c = {n: 0 for n in names}
        self.last = {}

    def mark(self, name, inst, n=1):
        inst.then_inc(self.h[name], n)
        self.c[name] += n
        return self.c[name]

    def wait2(self, eng, pair):
        self.wait(eng, pair[0], pair[1])

    def wait(self, eng, name, tok=None):
        if tok is None:
            tok = self.c[name]
        key = (id(eng), name)
        if self.last.get(key, -1) >= tok:
            return
        eng.wait_ge(self.h[name], tok)
        self.last[key] = tok


def build_nc():
    nc = bass.Bass("TRN2", target_bir_lowering=False)

    xc_d = nc.dram_tensor("xc", [TT, D], F32, kind="ExternalInput")
    xh_d = nc.dram_tensor("xh", [128, D], F32, kind="ExternalInput")
    hm_d = nc.dram_tensor("hm", [128, 1], F32, kind="ExternalInput")
    s0_d = nc.dram_tensor("s0", [128, 1], F32, kind="ExternalInput")
    s1_d = nc.dram_tensor("s1", [128, 1], F32, kind="ExternalInput")
    sca_d = nc.dram_tensor("sca", [128, 1], F32, kind="ExternalInput")
    w1_d = nc.dram_tensor("w1", [D, D], F32, kind="ExternalInput")
    w2p_d = nc.dram_tensor("w2p", [D, D], F32, kind="ExternalInput")
    wtwt_d = nc.dram_tensor("wtwt", [D, D], F32, kind="ExternalInput")
    wdwt_d = nc.dram_tensor("wdwt", [D, D], F32, kind="ExternalInput")
    wgt_d = nc.dram_tensor("wgt", [D, 6], F32, kind="ExternalInput")
    b1t_d = nc.dram_tensor("b1t", [128, 8], F32, kind="ExternalInput")
    b2r_d = nc.dram_tensor("b2r", [1, D], F32, kind="ExternalInput")
    wdbr_d = nc.dram_tensor("wdbr", [1, D], F32, kind="ExternalInput")
    wtbb_d = nc.dram_tensor("wtbb", [128, D], F32, kind="ExternalInput")
    mwt_d = nc.dram_tensor("mwt", [128, 8 * 41], F32, kind="ExternalInput")
    eye_d = nc.dram_tensor("eye32", [128, 32], BF, kind="ExternalInput")
    at_d = nc.dram_tensor("a_t", [16, 600], F32, kind="ExternalInput")
    bt_d = nc.dram_tensor("b_t", [16, 6144], F32, kind="ExternalInput")
    idb_d = nc.dram_tensor("idb", [128, 128], BF, kind="ExternalInput")
    idf_d = nc.dram_tensor("idf", [128, 128], F32, kind="ExternalInput")
    on_d = nc.dram_tensor("ones1", [1, 128], BF, kind="ExternalInput")
    xo_d = nc.dram_tensor("xo", [TT, D], F32, kind="ExternalOutput")
    cci_d = nc.dram_tensor("cc_in", [128, 16], F32)
    cco_d = nc.dram_tensor("cc_out", [128, 16], F32)

    es = contextlib.ExitStack()
    with es:
        sb = nc.sbuf_tensor
        psum = nc.psum_tensor
        ec = es.enter_context

        xbuf = [ec(sb(f"xbuf{i}", [128, D], F32)) for i in range(2)]
        qt = ec(sb("qt", [128, 8 * TT], BF))
        tfm = ec(sb("tfm", [128, 8 * 608], BF))
        val = ec(sb("val", [128, 5 * D], BF))
        wtbb = ec(sb("wtbb_s", [128, D], F32))
        mwt = ec(sb("mwt_s", [128, 8 * 41], F32))
        wp = ec(sb("wp_s", [128, 8 * 40], F32))
        b1t = ec(sb("b1t_s", [128, 8], F32))
        hm = ec(sb("hm_s", [128, 1], F32))
        s0 = ec(sb("s0_s", [128, 1], F32))
        s1 = ec(sb("s1_s", [128, 1], F32))
        sca = ec(sb("sca_s", [128, 1], F32))
        eye = ec(sb("eye_s", [128, 32], BF))
        idb = ec(sb("idb_s", [128, 128], BF))
        idf = ec(sb("idf_s", [128, 128], F32))
        ones = ec(sb("ones_s", [1, 128], BF))
        b2r = ec(sb("b2r_s", [1, D], F32))
        wdbrb = ec(sb("wdbr_b", [1, D], BF))
        part = ec(sb("part_s", [128, 8], F32))
        part4 = ec(sb("part4_s", [128, 32], F32))
        b1h = ec(sb("b1h_s", [128, 8], F32))
        ccs = ec(sb("ccs_s", [128, 16], F32))
        ccr = ec(sb("ccr_s", [128, 16], F32))
        tm1 = ec(sb("tm1_s", [128, 8], F32))
        tmine = ec(sb("tmine_s", [128, 8], F32))
        vsb = ec(sb("vsb_s", [128, 8], F32))
        vbf = ec(sb("vbf_s", [128, 8], BF))
        vrow = ec(sb("vrow_s", [1, D], F32))
        b2eb = ec(sb("b2e_b", [1, D], BF))
        epsc = ec(sb("eps_s", [128, 1], F32))
        stA = [ec(sb(f"stA{i}", [128, 16], F32)) for i in range(2)]

        S = Sems(nc, es, ["ld0", "ld1", "st0", "st1", "dwg", "dsm", "cc",
                          "sv", "ss", "sp", "sg"])
        blk = es.enter_context(nc.Block())

        @blk.sync
        def _(sync):
            V, SC, PE, G = nc.vector, nc.scalar, nc.tensor, nc.gpsimd
            SY = sync
            ld = ["ld0", "ld1"]
            st_ = ["st0", "st1"]

            for dst, src in [(b1t, b1t_d), (hm, hm_d), (s0, s0_d), (s1, s1_d),
                             (sca, sca_d), (eye, eye_d), (idb, idb_d), (idf, idf_d),
                             (ones, on_d), (b2r, b2r_d), (wtbb, wtbb_d),
                             (mwt, mwt_d)]:
                S.mark("dsm", SC.dma_start(out=dst[:, :], in_=src.ap()), 16)
            t_dsm = S.c["dsm"]

            t_eps = S.mark("sv", V.memset(epsc[:, :], EPS))

            t_xload = {}
            t_xfree = {}

            def load_xtile(idx, slot_idx, row0, halo=False):
                sl = ld[slot_idx]
                prev = t_xfree.get((slot_idx, "prev"))
                if prev is not None:
                    S.wait2(SY, prev)
                src = xh_d.ap() if halo else xc_d.ap()[row0:row0 + 128, :]
                t_xload[idx] = S.mark(sl, SY.dma_start(out=xbuf[slot_idx][:, :], in_=src), 16)

            t_qt_done = None
            with contextlib.ExitStack() as es_pa:
                p_sb = es_pa.enter_context(sb("p_sb", [128, 8 * W], BF))

                with contextlib.ExitStack() as es_a1:
                    lfm = es_a1.enter_context(sb("lfm", [128, 8 * W], BF))
                    lnb = [es_a1.enter_context(sb(f"lnb{i}", [128, D], BF)) for i in range(2)]
                    w1b = es_a1.enter_context(sb("w1b", [128, 8 * D], BF))
                    tpp = [es_a1.enter_context(psum(f"tpA{i}", [128, 128], BF)) for i in range(4)]
                    wps = [es_a1.enter_context(psum(f"wpsA{i}", [128, 512], F32)) for i in range(2)]

                    S.mark("dwg", G.dma_start(
                        out=w1b[:, :].rearrange("p (k d) -> p k d", k=8),
                        in_=w1_d.ap().rearrange("(k p) d -> p k d", p=128)), 16)
                    S.mark("dwg", G.dma_start(
                        out=tfm[:, :].rearrange("p (c e) -> p c e", c=8)[:, :, 600:606],
                        in_=wgt_d.ap().rearrange("(c p) e -> p c e", p=128)), 16)
                    S.mark("dwg", G.dma_start(out=wdbrb[:, :], in_=wdbr_d.ap()), 16)
                    t_dwg1 = S.c["dwg"]

                    t_tpev = {}
                    t_lnb_last = [0, 0]
                    t_lnbT = {}
                    t_lfm_ev = {}
                    S.wait(V, "dsm", t_dsm)
                    t_b1h = S.mark("sv", V.tensor_scalar(
                        b1h[:, :], b1t[:, :], hm[:, :], None, OP.mult))

                    load_xtile(0, 0, 0, halo=True)
                    load_xtile(1, 1, 0)

                    w1_pending = []

                    def w1_emit(cols0, n, is_halo, g, m):
                        pm = wps[m % 2]
                        prev = t_tpev.get(("w1", m % 2))
                        if prev is not None:
                            S.wait2(PE, prev)
                        S.wait(PE, "dwg", t_dwg1)
                        ii = None
                        for k in range(8):
                            ii = PE.matmul(
                                pm[:, 0:n],
                                w1b[:, 1024 * k + 128 * m: 1024 * k + 128 * (m + 1)],
                                lfm[:, W * k + cols0: W * k + cols0 + n],
                                start=(k == 0), stop=(k == 7))
                        t_mm = S.mark("sp", ii)
                        S.wait(SC, "sp", t_mm)
                        S.wait(SC, "dsm", t_dsm)
                        if is_halo:
                            S.wait(SC, "sv", t_b1h)
                            t_pe = S.mark("ss", SC.activation(
                                p_sb[:, W * m + cols0: W * m + cols0 + n],
                                pm[:, 0:n], AF.Identity,
                                scale=hm[:, :], bias=b1h[:, m:m + 1]))
                        else:
                            t_pe = S.mark("ss", SC.activation(
                                p_sb[:, W * m + cols0: W * m + cols0 + n],
                                pm[:, 0:n], AF.Identity,
                                bias=b1t[:, m:m + 1],
                                accum_out=part4[:, 4 * m + g:4 * m + g + 1]))
                        t_tpev[("w1", m % 2)] = ("ss", t_pe)

                    t_mv = {}
                    t_r = {}
                    t_ap = {}

                    def LNa(i):
                        pi = i % 2
                        xs = xbuf[pi]
                        stt = stA[pi]
                        S.wait(V, ld[pi], t_xload[i])
                        if i >= 2:
                            S.wait(V, "sv", t_lnb_last[pi])
                            S.wait(V, "ss", t_r[i - 2])
                        V.bn_stats(stt[:, 0:6], xs[:, 0:512])
                        t_s2 = S.mark("sv", V.bn_stats(stt[:, 6:12], xs[:, 512:1024]))
                        S.wait(V, "sv", t_s2)
                        t_mv[i] = S.mark("sv", V.bn_aggr(stt[:, 12:14], stt[:, 0:12]))

                    def LNb(i):
                        stt = stA[i % 2]
                        S.wait(SC, "sv", max(t_mv[i], t_eps))
                        t_l = S.mark("ss", SC.activation(stt[:, 14:15], stt[:, 13:14], AF.Ln, bias=epsc[:, :]))
                        S.wait(SC, "ss", t_l)
                        t_r[i] = S.mark("ss", SC.activation(stt[:, 14:15], stt[:, 14:15], AF.Exp, scale=-0.5))

                    def LNc(i):
                        pi = i % 2
                        stt = stA[pi]
                        S.wait(V, "ss", t_r[i])
                        if i >= 2:
                            S.wait(V, "sp", t_lnbT[i - 2])
                        t_ap[i] = S.mark("sv", V.tensor_scalar(
                            lnb[pi][:, :], xbuf[pi][:, :], stt[:, 12:13], stt[:, 14:15],
                            OP.subtract, OP.mult))
                        t_lnb_last[pi] = t_ap[i]
                        t_xfree[(pi, "prev")] = ("sv", t_ap[i])

                    def TR(i):
                        pi = i % 2
                        S.wait(PE, "sv", t_ap[i])
                        S.wait(PE, "dsm", t_dsm)
                        for c in range(8):
                            q = c % 4
                            prev = t_tpev.get(q)
                            if prev is not None:
                                S.wait2(PE, prev)
                            t_tp = S.mark("sp", PE.transpose(
                                tpp[q][:, :], lnb[pi][:, 128 * c:128 * (c + 1)], idb[:, :]))
                            dst = lfm[:, W * c + 128 * i: W * c + 128 * (i + 1)]
                            if c % 2 == 0:
                                S.wait(V, "sp", t_tp)
                                t_ev = ("sv", S.mark("sv", V.tensor_copy(dst, tpp[q][:, :])))
                            else:
                                S.wait(SC, "sp", t_tp)
                                t_ev = ("ss", S.mark("ss", SC.copy(dst, tpp[q][:, :])))
                            t_tpev[q] = t_ev
                            t_lfm_ev[(i, c)] = t_ev
                        t_lnbT[i] = S.c["sp"]

                    LNa(0)
                    LNb(0)
                    LNa(1)
                    for i in range(NT + 1):
                        LNc(i)
                        if i + 1 <= NT:
                            LNb(i + 1)
                        TR(i)
                        if i + 2 <= NT:
                            load_xtile(i + 2, (i + 2) % 2, 128 * (i + 1))
                            LNa(i + 2)
                        if i == 0:
                            S.wait2(PE, t_lfm_ev[(0, 6)])
                            S.wait2(PE, t_lfm_ev[(0, 7)])
                            w1_pending += [(0, 128, True, -1, m) for m in range(8)]
                        elif i >= 4 and i % 4 == 0:
                            S.wait2(PE, t_lfm_ev[(i, 6)])
                            S.wait2(PE, t_lfm_ev[(i, 7)])
                            w1_pending += [(128 + 512 * (i // 4 - 1), 512, False, i // 4 - 1, m)
                                           for m in range(8)]
                        budget = len(w1_pending) if i == NT else (4 if i == 0 else 2)
                        for _ in range(min(budget, len(w1_pending))):
                            w1_emit(*w1_pending.pop(0))

                    t_p_done_ss = S.c["ss"]
                    t_p_done_sv = S.c["sv"]
                    t_pe_a1 = S.c["sp"]

                    S.wait(V, "dsm", t_dsm)
                    t_wp = 0
                    for c in range(8):
                        t_wp = S.mark("sv", V.tensor_scalar(
                            wp[:, 40 * c:40 * (c + 1)], mwt[:, 41 * c + 1:41 * c + 41],
                            mwt[:, 41 * c:41 * c + 1], None, OP.subtract))
                    S.wait(V, "ss", t_p_done_ss)
                    t_part = S.mark("sv", V.tensor_reduce(
                        part[:, :], part4[:, :].rearrange("p (m g) -> p m g", m=8),
                        AX.X, OP.add))
                    S.wait(V, "sv", t_part)
                    V.tensor_scalar(ccs[:, 0:8], part[:, :], s0[:, :], None, OP.mult)
                    t_ccs = S.mark("sv", V.tensor_scalar(
                        ccs[:, 8:16], part[:, :], s1[:, :], None, OP.mult))

                w2b = ec(sb("w2b", [128, 8 * D], BF, side="right"))
                wdwtb = ec(sb("wdwtb", [128, 8 * D], BF, side="right"))
                with contextlib.ExitStack() as es_a2:
                    diagsA = es_a2.enter_context(sb("diagsA", [128, 2 * 5120], BF))
                    wtwtb = es_a2.enter_context(sb("wtwtb", [128, 8 * D], BF))
                    atb = es_a2.enter_context(sb("a_tb", [16, 600], BF))
                    btb = es_a2.enter_context(sb("b_tb", [16, 6144], BF))

                    S.wait(G, "sp", t_pe_a1)
                    for dst, srcd in [(w2b, w2p_d), (wdwtb, wdwt_d), (wtwtb, wtwt_d)]:
                        S.mark("dwg", G.dma_start(
                            out=dst[:, :].rearrange("p (k d) -> p k d", k=8),
                            in_=srcd.ap().rearrange("(k p) d -> p k d", p=128)), 16)
                    S.mark("dwg", G.dma_start(out=atb[:, :], in_=at_d.ap()), 16)
                    S.mark("dwg", G.dma_start(out=btb[:, :], in_=bt_d.ap()), 16)
                    S.wait(G, "sv", t_ccs)
                    S.mark("dwg", G.dma_start(out=cci_d.ap(), in_=ccs[:, :]), 16)
                    t_dwg2 = S.c["dwg"]
                    S.wait(G, "dwg", S.c["dwg"])
                    t_cc = S.mark("cc", G.collective_compute(
                        "AllReduce", OP.add, replica_groups=[list(range(8))],
                        ins=[cci_d.ap()], outs=[cco_d.ap()]))
                    S.wait(G, "cc", t_cc)
                    S.mark("dwg", G.dma_start(out=ccr[:, :], in_=cco_d.ap()), 16)
                    t_dwg_cc = S.c["dwg"]

                    t_diag = [0, 0]
                    for s in range(2):
                        S.wait(V, "sv", t_wp)
                        S.wait(V, "dsm", t_dsm)
                        tc = S.mark("sv", V.tensor_copy(
                            diagsA[:, 5120 * s: 5120 * (s + 1)].rearrange(
                                "p (j k q) -> p j k q", j=4, k=40),
                            wp[:, :].rearrange("p (c k) -> p c k", c=8)[:, 4 * s:4 * s + 4, :]
                            .broadcast_to([128, 4, 40, 32])))
                        S.wait(V, "sv", tc)
                        t_diag[s] = S.mark("sv", V.tensor_tensor(
                            diagsA[:, 5120 * s: 5120 * (s + 1)].rearrange("p (r q) -> p r q", r=160),
                            diagsA[:, 5120 * s: 5120 * (s + 1)].rearrange("p (r q) -> p r q", r=160),
                            eye[:, :].unsqueeze(1).broadcast_to([128, 160, 32]),
                            OP.mult))

                    with contextlib.ExitStack() as es_cv:
                        cv = [es_cv.enter_context(psum(f"cv{i}", [128, 512], F32)) for i in range(8)]
                        t_cv_ev = {}
                        S.wait(PE, "ss", t_p_done_ss)
                        S.wait(PE, "sv", t_p_done_sv)
                        for s in range(2):
                            S.wait(PE, "sv", t_diag[s])
                            for tg in range(4):
                                pset = (s * 4 + tg) % 2
                                for i4 in range(4):
                                    prev = t_cv_ev.get((pset, i4))
                                    if prev is not None:
                                        S.wait2(PE, prev)
                                ii = None
                                for k in range(TAPS):
                                    for j4 in range(4):
                                        for i4 in range(4):
                                            base = 5120 * s + (40 * j4 + k) * 32
                                            mv0 = W * (4 * s + j4) + 128 + 512 * tg - k
                                            ii = PE.matmul(
                                                cv[4 * pset + i4][32 * j4:32 * (j4 + 1), :],
                                                diagsA[32 * i4:32 * (i4 + 1), base:base + 32],
                                                p_sb[32 * i4:32 * (i4 + 1), mv0:mv0 + 512],
                                                start=(k == 0), stop=(k == TAPS - 1),
                                                tile_position=(32 * i4, 32 * j4),
                                                skip_group_check=True)
                                t_mm = S.mark("sp", ii)
                                for i4 in range(4):
                                    dst = qt[:, TT * (4 * s + i4) + 512 * tg:
                                             TT * (4 * s + i4) + 512 * (tg + 1)]
                                    if i4 % 2 == 0:
                                        S.wait(V, "sp", t_mm)
                                        t_cv_ev[(pset, i4)] = ("sv", S.mark(
                                            "sv", V.tensor_copy(dst, cv[4 * pset + i4][:, :])))
                                    else:
                                        S.wait(SC, "sp", t_mm)
                                        t_cv_ev[(pset, i4)] = ("ss", S.mark(
                                            "ss", SC.copy(dst, cv[4 * pset + i4][:, :])))
                        t_qt_done = S.c["sv"]
                        t_qt_done_ss = S.c["ss"]
                    with contextlib.ExitStack() as es_b1:
                        bld = [es_b1.enter_context(psum(f"bld{i}", [128, 1024], F32)) for i in range(2)]
                        S.wait(PE, "sv", t_qt_done)
                        S.wait(PE, "ss", t_qt_done_ss)
                        S.wait(PE, "dwg", t_dwg2)
                        t_tfm_ev = [0] * 8
                        for c in range(8):
                            pm = bld[c % 2]
                            if c >= 2:
                                S.wait(PE, "sv", t_tfm_ev[c - 2])
                            ii = None
                            for e in range(6):
                                ii = PE.matmul(pm[:, 128 * e:128 * e + 100],
                                               btb[:, 1024 * e + 128 * c: 1024 * e + 128 * (c + 1)],
                                               atb[:, 100 * e:100 * (e + 1)],
                                               start=True, stop=True)
                            t_mm = S.mark("sp", ii)
                            S.wait(V, "sp", t_mm)
                            t_tfm_ev[c] = S.mark("sv", V.tensor_copy(
                                tfm[:, 608 * c: 608 * c + 600].rearrange("p (e l) -> p e l", e=6),
                                pm[:, :].rearrange("p (e l) -> p e l", e=8)[:, 0:6, 0:100]))
                        t_tfm_done = t_tfm_ev[7]

                        S.wait(PE, "dwg", t_dwg2)
                        t_zv = S.mark("sv", V.memset(val[64:128, D * 4:D * 5], 0.0))
                        t_val_ev = [0] * 10
                        vi = 0
                        for v in range(5):
                            mrows = 128 if v < 4 else 88
                            for jh in range(2):
                                pm = bld[vi % 2]
                                if vi >= 2:
                                    S.wait(PE, "sv", t_val_ev[vi - 2])
                                S.wait(PE, "sv", t_tfm_done)
                                ii = None
                                for k in range(8):
                                    ii = PE.matmul(
                                        pm[0:mrows, 0:512],
                                        tfm[:, 608 * k + 128 * v: 608 * k + 128 * v + mrows],
                                        wtwtb[:, 1024 * k + 512 * jh: 1024 * k + 512 * (jh + 1)],
                                        start=(k == 0), stop=(k == 7))
                                t_mm = S.mark("sp", ii)
                                S.wait(V, "sp", t_mm)
                                S.wait(V, "dsm", t_dsm)
                                S.wait(V, "sv", t_zv)
                                t_val_ev[vi] = S.mark("sv", V.tensor_tensor(
                                    val[0:mrows, D * v + 512 * jh: D * v + 512 * (jh + 1)],
                                    pm[0:mrows, 0:512], wtbb[0:mrows, 512 * jh:512 * (jh + 1)],
                                    OP.add))
                                vi += 1
                        t_val_done = max(t_val_ev[9], t_zv)

                        S.wait(V, "dwg", t_dwg_cc)
                        t_t1 = S.mark("sv", V.tensor_scalar(
                            tm1[:, :], ccr[:, 8:16], s1[:, :], None, OP.mult))
                        S.wait(V, "sv", t_t1)
                        t_tm = S.mark("sv", V.scalar_tensor_tensor(
                            tmine[:, :], ccr[:, 0:8], s0[:, :], tm1[:, :], OP.mult, OP.add))
                        S.wait(V, "sv", t_tm)
                        t_v = S.mark("sv", V.tensor_tensor(
                            vsb[:, :], tmine[:, :],
                            mwt[:, :].rearrange("p (c k) -> p c k", c=8)[:, :, 0:1]
                            .rearrange("p c k -> p (c k)"),
                            OP.mult))
                        S.wait(V, "sv", t_v)
                        t_vb = S.mark("sv", V.tensor_copy(vbf[:, :], vsb[:, :]))
                        S.wait(PE, "sv", max(t_vb, t_val_ev[9]))
                        S.wait(PE, "dwg", t_dwg2)
                        t_vre = [0, 0]
                        for jh in range(2):
                            pm = bld[jh]
                            ii = None
                            for k in range(8):
                                ii = PE.matmul(pm[0:1, 0:512], vbf[:, k:k + 1],
                                               w2b[:, 1024 * k + 512 * jh: 1024 * k + 512 * (jh + 1)],
                                               start=(k == 0), stop=(k == 7))
                            t_mm = S.mark("sp", ii)
                            S.wait(V, "sp", t_mm)
                            t_vre[jh] = S.mark("sv", V.tensor_copy(
                                vrow[0:1, 512 * jh:512 * (jh + 1)], pm[0:1, 0:512]))
                        S.wait(V, "sv", max(t_vre))
                        S.wait(V, "dsm", t_dsm)
                        t_vr = S.mark("sv", V.tensor_tensor(
                            vrow[0:1, :], vrow[0:1, :], b2r[0:1, :], OP.add))
                        S.wait(V, "sv", t_vr)
                        t_b2e = S.mark("sv", V.tensor_copy(b2eb[0:1, :], vrow[0:1, :]))


            with contextlib.ExitStack() as es_b:
                x2fm = [es_b.enter_context(sb(f"x2fm{i}", [128, D], BF)) for i in range(2)]
                xmid = [es_b.enter_context(sb(f"xmid{i}", [128, D], F32)) for i in range(2)]
                x2f = [es_b.enter_context(sb(f"x2f{i}", [128, D], F32)) for i in range(2)]
                o2s = [es_b.enter_context(sb(f"o2s{i}", [128, D], F32)) for i in range(2)]
                m600 = [es_b.enter_context(sb(f"m600_{i}", [128, 640], F32)) for i in range(2)]
                m600t = [es_b.enter_context(sb(f"m600t{i}", [128, 640], BF)) for i in range(2)]
                asel = [es_b.enter_context(sb(f"asel{i}", [128, 100], F32)) for i in range(2)]
                aexp = [es_b.enter_context(sb(f"aexp{i}", [128, 100], F32)) for i in range(2)]
                dffm = [es_b.enter_context(sb(f"dffm{i}", [128, D], BF)) for i in range(2)]
                fout = [es_b.enter_context(sb(f"fout{i}", [128, D], F32)) for i in range(2)]
                xos = [es_b.enter_context(sb(f"xos{i}", [128, D], F32)) for i in range(2)]
                stB = [es_b.enter_context(sb(f"stB{i}", [128, 16], F32)) for i in range(2)]
                st3 = [es_b.enter_context(sb(f"st3_{i}", [128, 16], F32)) for i in range(2)]
                gB = [es_b.enter_context(sb(f"gB{i}", [128, 24], F32)) for i in range(2)]
                gs6 = [es_b.enter_context(sb(f"gs6_{i}", [128, 6], F32)) for i in range(2)]
                pAD = [es_b.enter_context(psum(f"pAD{i}", [128, 1024], F32)) for i in range(2)]
                pSF = [es_b.enter_context(psum(f"pSF{i}", [128, 1024], F32)) for i in range(2)]

                S.wait(V, "sv", S.c["sv"])
                S.wait(V, "ss", S.c["ss"])
                S.wait(V, "sp", S.c["sp"])
                t_ms = [S.mark("sv", V.memset(m600[i][:, :], 0.0)) for i in range(2)]
                t_store_cnt = [0, 0]
                TE = {}
                tok = {}

                load_xtile(("b", 0), 0, 0)
                load_xtile(("b", 1), 1, 128)

                def ln_chain(stt, src_ap, nmr=True):
                    V.bn_stats(stt[:, 0:6], src_ap[:, 0:512])
                    t_s2 = S.mark("sv", V.bn_stats(stt[:, 6:12], src_ap[:, 512:1024]))
                    S.wait(V, "sv", t_s2)
                    t_mv = S.mark("sv", V.bn_aggr(stt[:, 12:14], stt[:, 0:12]))
                    S.wait(SC, "sv", max(t_mv, t_eps))
                    t_l = S.mark("ss", SC.activation(stt[:, 14:15], stt[:, 13:14], AF.Ln, bias=epsc[:, :]))
                    S.wait(SC, "ss", t_l)
                    t_r = S.mark("ss", SC.activation(stt[:, 14:15], stt[:, 14:15], AF.Exp, scale=-0.5))
                    t_nmr = None
                    if nmr:
                        S.wait(V, "ss", t_r)
                        t_nmr = S.mark("sv", V.tensor_scalar(
                            stt[:, 15:16], stt[:, 12:13], stt[:, 14:15], -1.0, OP.mult, OP.mult))
                    return t_r, t_nmr

                def fPE(tb):
                    pi = tb % 2
                    if tb >= 2:
                        for eng in (V, SC, PE, G):
                            for snm in ("sv", "ss", "sp", "sg"):
                                S.wait(eng, snm, TE[tb - 2][snm])
                    if tb >= 1:
                        for pr in tok[(tb - 1, "x2Tev_AD")]:
                            S.wait2(PE, pr)
                    if tb == 0:
                        S.wait(PE, "sv", max(t_qt_done, t_b2e))
                        S.wait(PE, "ss", t_qt_done_ss)
                        S.wait(PE, "dwg", t_dwg2)
                    ii = None
                    for jh in range(2):
                        for k in range(8):
                            ii = PE.matmul(
                                pAD[pi][:, 512 * jh:512 * (jh + 1)],
                                qt[:, TT * k + 128 * tb: TT * k + 128 * (tb + 1)],
                                w2b[:, 1024 * k + 512 * jh: 1024 * k + 512 * (jh + 1)],
                                start=(k == 0), stop=False)
                        ii = PE.matmul(pAD[pi][:, 512 * jh:512 * (jh + 1)],
                                       ones[0:1, :], b2eb[0:1, 512 * jh:512 * (jh + 1)],
                                       start=False, stop=True)
                    tok[(tb, "out2")] = S.mark("sp", ii)

                def fVa(tb):
                    pi = tb % 2
                    stt = stB[pi]
                    S.wait(V, "sp", tok[(tb, "out2")])
                    _, t_nmr2 = ln_chain(stt, pAD[pi])
                    S.wait(SC, "sp", tok[(tb, "out2")])
                    S.wait(SC, "sv", t_nmr2)
                    t_t2 = S.mark("ss", SC.activation(
                        o2s[pi][:, :], pAD[pi][:, :], AF.Identity,
                        scale=stt[:, 14:15], bias=stt[:, 15:16]))
                    tok[(tb, "pAfree")] = ("ss", t_t2)
                    S.wait(V, "ss", t_t2)
                    S.wait(V, ld[pi], t_xload[("b", tb)])
                    t_xmid = S.mark("sv", V.tensor_tensor(
                        xmid[pi][:, :], o2s[pi][:, :], xbuf[pi][:, :], OP.add))
                    t_xfree[(pi, "prev")] = ("sv", t_xmid)
                    tok[(tb, "xmid")] = t_xmid
                    if tb + 2 < NT:
                        load_xtile(("b", tb + 2), (tb + 2) % 2, 128 * (tb + 2))

                def fVb(tb):
                    pi = tb % 2
                    S.wait(V, "sv", tok[(tb, "xmid")])
                    t_r3, _ = ln_chain(st3[pi], xmid[pi], nmr=False)
                    S.wait(V, "ss", t_r3)
                    t_x2 = S.mark("sv", V.tensor_scalar(
                        x2f[pi][:, :], xmid[pi][:, :], st3[pi][:, 12:13], st3[pi][:, 14:15],
                        OP.subtract, OP.mult))
                    tok[(tb, "x2")] = t_x2

                def midT(tb):
                    pi = tb % 2
                    S.wait(PE, "sv", tok[(tb, "x2")])
                    if tb >= 1:
                        S.wait2(PE, tok[(tb - 1, "pDfree")])
                    if tb >= 1:
                        for pr in tok[(tb - 1, "m6tev")]:
                            S.wait2(PE, pr)
                    tpev = [None, None, None, None]
                    for c in range(8):
                        q = c % 4
                        slot = (pSF[pi] if q < 2 else pAD[1 - pi])
                        col = 512 * (q % 2)
                        if tpev[q] is not None:
                            S.wait2(PE, tpev[q])
                        t_tp = S.mark("sp", PE.transpose(
                            slot[:, col:col + 128], x2f[pi][:, 128 * c:128 * (c + 1)], idf[:, :]))
                        dst = x2fm[pi][:, 128 * c:128 * (c + 1)]
                        if c % 2 == 0:
                            S.wait(SC, "sp", t_tp)
                            tpev[q] = ("ss", S.mark("ss", SC.copy(dst, slot[:, col:col + 128])))
                        else:
                            S.wait(V, "sp", t_tp)
                            tpev[q] = ("sv", S.mark("sv", V.tensor_copy(dst, slot[:, col:col + 128])))
                    tok[(tb, "x2Tev_AD")] = [tpev[2], tpev[3]]
                    tok[(tb, "x2fm_v")] = S.c["sv"]
                    tok[(tb, "x2fm_s")] = S.c["ss"]

                def midR(tb):
                    pi = tb % 2
                    g = gB[pi]
                    S.wait(PE, "sv", max(tok[(tb, "x2fm_v")], t_tfm_done))
                    S.wait(PE, "ss", tok[(tb, "x2fm_s")])
                    ii = None
                    for k in range(8):
                        ii = PE.matmul(pSF[pi][:, 512:606],
                                       x2fm[pi][:, 128 * k:128 * (k + 1)],
                                       tfm[:, 608 * k + 512: 608 * k + 606],
                                       start=(k == 0), stop=(k == 7))
                    t_sc2 = S.mark("sp", ii)
                    for k in range(8):
                        ii = PE.matmul(pSF[pi][:, 0:512],
                                       x2fm[pi][:, 128 * k:128 * (k + 1)],
                                       tfm[:, 608 * k: 608 * k + 512],
                                       start=(k == 0), stop=(k == 7))
                    t_sc = S.mark("sp", ii)

                    S.wait2(PE, tok[(tb, "pAfree")])
                    for jh in range(2):
                        for k in range(8):
                            PE.matmul(
                                pAD[pi][:, 512 * jh:512 * (jh + 1)],
                                x2fm[pi][:, 128 * k:128 * (k + 1)],
                                wdwtb[:, 1024 * k + 512 * jh: 1024 * k + 512 * (jh + 1)],
                                start=(k == 0), stop=False, skip_group_check=True)

                    S.wait(V, "sp", t_sc2)
                    t_m6 = S.mark("sv", V.tensor_reduce(
                        g[:, 0:1], pSF[pi][:, 600:606], AX.X, OP.max))
                    S.wait(V, "sv", t_m6)
                    t_hits = S.mark("sv", V.tensor_scalar(
                        g[:, 1:7], pSF[pi][:, 600:606], g[:, 0:1], None, OP.is_ge))
                    S.wait(V, "sv", t_hits)
                    t_cum = S.mark("sv", V.tensor_tensor_scan(
                        g[:, 7:13], g[:, 1:7], g[:, 1:7], 0.0, OP.add, OP.bypass))
                    S.wait(V, "sv", t_cum)
                    t_g6 = S.mark("sv", V.scalar_tensor_tensor(
                        g[:, 13:19], g[:, 7:13], 1.0, g[:, 1:7], OP.is_equal, OP.mult))
                    S.wait(V, "sv", t_g6)
                    S.wait(V, "sp", t_sc)
                    t_a0 = S.mark("sv", V.tensor_scalar(
                        asel[pi][:, :], pSF[pi][:, 0:100], g[:, 13:14], None, OP.mult))
                    t_a1 = S.mark("sv", V.tensor_scalar(
                        aexp[pi][:, :], pSF[pi][:, 100:200], g[:, 14:15], None, OP.mult))
                    for e in range(2, 6):
                        if e % 2 == 0:
                            S.wait(V, "sv", t_a0)
                            t_a0 = S.mark("sv", V.scalar_tensor_tensor(
                                asel[pi][:, :], pSF[pi][:, 100 * e:100 * (e + 1)], g[:, 13 + e:14 + e],
                                asel[pi][:, :], OP.mult, OP.add))
                        else:
                            S.wait(V, "sv", t_a1)
                            t_a1 = S.mark("sv", V.scalar_tensor_tensor(
                                aexp[pi][:, :], pSF[pi][:, 100 * e:100 * (e + 1)], g[:, 13 + e:14 + e],
                                aexp[pi][:, :], OP.mult, OP.add))
                    S.wait(V, "sv", max(t_a0, t_a1))
                    t_as = S.mark("sv", V.tensor_tensor(
                        asel[pi][:, :], asel[pi][:, :], aexp[pi][:, :], OP.add))
                    tok[(tb, "gather")] = t_as
                    S.wait(V, "sv", t_as)
                    t_am = S.mark("sv", V.tensor_reduce(g[:, 19:20], asel[pi][:, :], AX.X, OP.max))
                    S.wait(V, "sv", t_am)
                    t_nbm = S.mark("sv", V.tensor_scalar(
                        g[:, 20:21], g[:, 19:20], -BETA, None, OP.mult))
                    S.wait(SC, "sv", max(t_nbm, t_as))
                    t_ae = S.mark("ss", SC.activation(
                        aexp[pi][:, :], asel[pi][:, :], AF.Exp,
                        bias=g[:, 20:21], scale=BETA, accum_out=g[:, 21:22]))
                    S.wait(V, "ss", t_ae)
                    t_rs = S.mark("sv", V.reciprocal(g[:, 22:23], g[:, 21:22]))
                    S.wait(V, "sv", max(t_rs, t_ms[pi]))
                    t_grs = S.mark("sv", V.tensor_scalar(
                        gs6[pi][:, :], g[:, 13:19], g[:, 22:23], None, OP.mult))
                    S.wait(SC, "sv", t_grs)
                    S.wait(V, "sv", t_grs)
                    S.wait(SC, "ss", t_ae)
                    sct = []
                    for e in range(6):
                        if e % 2 == 0:
                            sct.append(("sv", S.mark("sv", V.tensor_scalar(
                                m600[pi][:, 100 * e + 1:100 * (e + 1)],
                                aexp[pi][:, 1:100], gs6[pi][:, e:e + 1], None, OP.mult))))
                        else:
                            sct.append(("ss", S.mark("ss", SC.activation(
                                m600[pi][:, 100 * e + 1:100 * (e + 1)],
                                aexp[pi][:, 1:100], AF.Identity, scale=gs6[pi][:, e:e + 1]))))
                    tok[(tb, "scatter")] = sct

                def backP1a(tb):
                    pi = tb % 2
                    S.wait(PE, "sv", tok[(tb, "gather")])
                    if tb >= 1:
                        S.wait2(PE, tok[(tb - 1, "dffm")])
                    sct = tok[(tb, "scatter")]
                    tpev = [None, None, None, None]
                    for c in range(5):
                        q = c % 4
                        slot = (pSF[pi] if q < 2 else pSF[1 - pi])
                        col = 512 * (q % 2)
                        if tpev[q] is not None:
                            S.wait2(PE, tpev[q])
                        mx = {}
                        for e in range(6):
                            if 100 * e + 1 < 128 * (c + 1) and 100 * (e + 1) > 128 * c:
                                mx[sct[e][0]] = sct[e]
                        for pr in mx.values():
                            S.wait2(PE, pr)
                        t_tp = S.mark("sp", PE.transpose(
                            slot[:, col:col + 128], m600[pi][:, 128 * c:128 * (c + 1)], idf[:, :]))
                        dst = m600t[pi][:, 128 * c:128 * (c + 1)]
                        if c % 2 == 0:
                            S.wait(SC, "sp", t_tp)
                            tpev[q] = ("ss", S.mark("ss", SC.copy(dst, slot[:, col:col + 128])))
                        else:
                            S.wait(V, "sp", t_tp)
                            tpev[q] = ("sv", S.mark("sv", V.tensor_copy(dst, slot[:, col:col + 128])))
                    tok[(tb, "m6tev")] = [("ss", S.c["ss"]), ("sv", S.c["sv"])]
                    tok[(tb, "m6t_v")] = S.c["sv"]
                    tok[(tb, "m6t_s")] = S.c["ss"]

                def backP1b(tb):
                    pi = tb % 2
                    S.wait(PE, "sv", max(tok[(tb, "m6t_v")], t_val_done))
                    S.wait(PE, "ss", tok[(tb, "m6t_s")])
                    ii = None
                    for m in range(8):
                        for kel in range(5):
                            ii = PE.matmul(
                                pSF[pi][:, 128 * m:128 * (m + 1)],
                                val[:, D * kel + 128 * m: D * kel + 128 * (m + 1)],
                                m600t[pi][:, 128 * kel:128 * (kel + 1)],
                                start=(kel == 0), stop=(kel == 4),
                                skip_group_check=True)
                    t_df = S.mark("sp", ii)
                    S.wait(SC, "sp", t_df)
                    t_dfb = S.mark("ss", SC.copy(dffm[pi][:, :], pSF[pi][:, :]))
                    tok[(tb, "dffm")] = ("ss", t_dfb)

                def backP2(tb):
                    pi = tb % 2
                    stt = stB[pi]
                    S.wait2(PE, tok[(tb, "dffm")])
                    S.wait(PE, "dwg", t_dwg_cc)
                    ii = None
                    for jh in range(2):
                        for k in range(8):
                            ii = PE.matmul(
                                pAD[pi][:, 512 * jh:512 * (jh + 1)],
                                dffm[pi][:, 128 * k:128 * (k + 1)],
                                wdwtb[:, 1024 * k + 512 * jh: 1024 * k + 512 * (jh + 1)],
                                start=False, stop=False, skip_group_check=True)
                        ii = PE.matmul(pAD[pi][:, 512 * jh:512 * (jh + 1)],
                                       ones[0:1, :], wdbrb[0:1, 512 * jh:512 * (jh + 1)],
                                       start=False, stop=True, skip_group_check=True)
                    t_dl = S.mark("sp", ii)

                    S.wait(SC, "sp", t_dl)
                    if t_store_cnt[pi] > 0:
                        S.wait(SC, st_[pi], 16 * t_store_cnt[pi])
                    t_t5 = S.mark("ss", SC.activation(
                        xos[pi][:, :], pAD[pi][:, :], AF.Identity, scale=sca[:, :]))
                    tok[(tb, "pDfree")] = ("ss", t_t5)
                    S.wait(V, "ss", t_t5)
                    S.wait(V, "sv", tok[(tb, "x2")])
                    t_fo = S.mark("sv", V.tensor_tensor(
                        fout[pi][:, :], xos[pi][:, :], x2f[pi][:, :], OP.add))
                    tok[(tb, "fout")] = t_fo

                def backP3(tb):
                    pi = tb % 2
                    stt = stB[pi]
                    S.wait(V, "sv", tok[(tb, "fout")])
                    _, t_nmr4 = ln_chain(stt, fout[pi])
                    S.wait(SC, "sv", t_nmr4)
                    t_t4 = S.mark("ss", SC.activation(
                        o2s[pi][:, :], fout[pi][:, :], AF.Identity,
                        scale=stt[:, 14:15], bias=stt[:, 15:16]))
                    S.wait(G, "ss", t_t4)
                    S.wait(G, "sv", tok[(tb, "xmid")])
                    t_xo = S.mark("sg", G.tensor_tensor(
                        xos[pi][:, :], o2s[pi][:, :], xmid[pi][:, :], OP.add))
                    S.wait(SY, "sg", t_xo)
                    S.mark(st_[pi], SY.dma_start(
                        out=xo_d.ap()[128 * tb:128 * (tb + 1), :], in_=xos[pi][:, :]), 16)
                    t_store_cnt[pi] += 1
                    TE[tb] = {snm: S.c[snm] for snm in ("sv", "ss", "sp", "sg")}

                fPE(0)
                fVa(0)
                fVb(0)
                midT(0)
                midR(0)
                for tb in range(NT):
                    if tb + 1 < NT:
                        fPE(tb + 1)
                    backP1a(tb)
                    if tb + 1 < NT:
                        fVa(tb + 1)
                    backP1b(tb)
                    if tb + 1 < NT:
                        fVb(tb + 1)
                    backP2(tb)
                    if tb + 1 < NT:
                        midT(tb + 1)
                    backP3(tb)
                    if tb + 1 < NT:
                        midR(tb + 1)

                S.wait(SY, "st0", 16 * t_store_cnt[0])
                S.wait(SY, "st1", 16 * t_store_cnt[1])
    return nc


_NC_CACHE = {}


def _get_nc():
    if "nc" not in _NC_CACHE:
        _NC_CACHE["nc"] = build_nc()
    return _NC_CACHE["nc"]


def kernel(L0_attn_input_data, L0_attn_W1, L0_attn_bias1, L0_attn_W2,
           L0_attn_bias2, L0_attn_memory_weights, L0_ffn_A, L0_ffn_B,
           L0_ffn_w_gate, L0_ffn_wt_w, L0_ffn_wt_b, L0_ffn_wd_w, L0_ffn_wd_b,
           L0_ffn_scale):
    x = np.ascontiguousarray(np.asarray(L0_attn_input_data, dtype=np.float32))
    W1 = np.ascontiguousarray(np.asarray(L0_attn_W1, dtype=np.float32))
    b1 = np.asarray(L0_attn_bias1, dtype=np.float32)
    W2 = np.ascontiguousarray(np.asarray(L0_attn_W2, dtype=np.float32))
    b2 = np.asarray(L0_attn_bias2, dtype=np.float32)
    mw = np.ascontiguousarray(np.asarray(L0_attn_memory_weights, dtype=np.float32))
    A = np.ascontiguousarray(np.asarray(L0_ffn_A, dtype=np.float32))
    Bm = np.ascontiguousarray(np.asarray(L0_ffn_B, dtype=np.float32))
    wg = np.ascontiguousarray(np.asarray(L0_ffn_w_gate, dtype=np.float32))
    wtw = np.ascontiguousarray(np.asarray(L0_ffn_wt_w, dtype=np.float32))
    wtb = np.asarray(L0_ffn_wt_b, dtype=np.float32)
    wdw = np.ascontiguousarray(np.asarray(L0_ffn_wd_w, dtype=np.float32))
    wdb = np.asarray(L0_ffn_wd_b, dtype=np.float32)
    sc = np.asarray(L0_ffn_scale, dtype=np.float32)

    perm = []
    for s in range(2):
        for i in range(4):
            for j in range(4):
                base = 32 * (16 * s + 4 * j + i)
                perm.extend(range(base, base + 32))
    w2p = np.ascontiguousarray(W2[np.asarray(perm), :])

    shared = {
        "w1": W1,
        "w2p": w2p,
        "wtwt": np.ascontiguousarray(wtw.T),
        "wdwt": np.ascontiguousarray(wdw.T),
        "wgt": wg,
        "b1t": np.ascontiguousarray(b1.reshape(8, 128).T),
        "b2r": b2.reshape(1, D),
        "wdbr": wdb.reshape(1, D),
        "wtbb": np.ascontiguousarray(np.tile(wtb.reshape(1, D), (128, 1))),
        "mwt": np.ascontiguousarray(
            mw.T.reshape(8, 128, 41).transpose(1, 0, 2).reshape(128, 8 * 41)),
        "eye32": np.eye(32, dtype=BFNP)[np.tile(np.arange(32), 4)].reshape(128, 32),
        "a_t": np.ascontiguousarray(A.transpose(2, 0, 1).reshape(16, 600)),
        "b_t": np.ascontiguousarray(Bm.transpose(1, 0, 2).reshape(16, 6144)),
        "idb": np.eye(128, dtype=BFNP),
        "idf": np.eye(128, dtype=np.float32),
        "ones1": np.ones((1, 128), dtype=BFNP),
        "sca": np.tile(sc.reshape(1, 1), (128, 1)).astype(np.float32),
    }

    in_maps = []
    for core in range(8):
        b = core // 4
        j = core % 4
        t0 = 2048 * j
        m = {"xc": np.ascontiguousarray(x[b, t0:t0 + 2048])}
        if j == 0:
            m["xh"] = np.zeros((128, D), dtype=np.float32)
            m["hm"] = np.zeros((128, 1), dtype=np.float32)
        else:
            m["xh"] = np.ascontiguousarray(x[b, t0 - 128:t0])
            m["hm"] = np.ones((128, 1), dtype=np.float32)
        m["s0"] = np.full((128, 1), 1.0 if b == 0 else 0.0, dtype=np.float32)
        m["s1"] = np.full((128, 1), 1.0 if b == 1 else 0.0, dtype=np.float32)
        m.update(shared)
        in_maps.append(m)

    nc = _get_nc()
    res = run_bass_kernel_spmd(nc, in_maps, core_ids=list(range(8)))
    out = np.empty((2, 8192, D), dtype=np.float32)
    for core in range(8):
        b = core // 4
        j = core % 4
        out[b, 2048 * j:2048 * (j + 1)] = res.results[core]["xo"]
    return out


# version 28
# speedup vs baseline: 1.0525x; 1.0525x over previous
import contextlib
import numpy as np
import ml_dtypes

import concourse.bass as bass
import concourse.mybir as mybir
from concourse.bass_utils import run_bass_kernel_spmd

F32 = mybir.dt.float32
BF = mybir.dt.bfloat16
AF = mybir.ActivationFunctionType
OP = mybir.AluOpType
AX = mybir.AxisListType
BFNP = ml_dtypes.bfloat16

D = 1024
NCH = 8
NT = 16
TT = NT * 128
W = TT + 128
TAPS = 40
EPS = 1e-5
BETA = float(D) ** -0.5


class Sems:
    def __init__(self, nc, es, names):
        self.h = {n: es.enter_context(nc.semaphore(n)) for n in names}
        self.c = {n: 0 for n in names}
        self.last = {}

    def mark(self, name, inst, n=1):
        inst.then_inc(self.h[name], n)
        self.c[name] += n
        return self.c[name]

    def wait2(self, eng, pair):
        self.wait(eng, pair[0], pair[1])

    def wait(self, eng, name, tok=None):
        if tok is None:
            tok = self.c[name]
        key = (id(eng), name)
        if self.last.get(key, -1) >= tok:
            return
        eng.wait_ge(self.h[name], tok)
        self.last[key] = tok


def build_nc():
    nc = bass.Bass("TRN2", target_bir_lowering=False)

    xc_d = nc.dram_tensor("xc", [TT, D], F32, kind="ExternalInput")
    xh_d = nc.dram_tensor("xh", [128, D], F32, kind="ExternalInput")
    hm_d = nc.dram_tensor("hm", [128, 1], F32, kind="ExternalInput")
    s0_d = nc.dram_tensor("s0", [128, 1], F32, kind="ExternalInput")
    s1_d = nc.dram_tensor("s1", [128, 1], F32, kind="ExternalInput")
    sca_d = nc.dram_tensor("sca", [128, 1], F32, kind="ExternalInput")
    w1_d = nc.dram_tensor("w1", [D, D], F32, kind="ExternalInput")
    w2p_d = nc.dram_tensor("w2p", [D, D], F32, kind="ExternalInput")
    wtwt_d = nc.dram_tensor("wtwt", [D, D], F32, kind="ExternalInput")
    wdwt_d = nc.dram_tensor("wdwt", [D, D], F32, kind="ExternalInput")
    wgt_d = nc.dram_tensor("wgt", [D, 6], F32, kind="ExternalInput")
    b1t_d = nc.dram_tensor("b1t", [128, 8], F32, kind="ExternalInput")
    b2r_d = nc.dram_tensor("b2r", [1, D], F32, kind="ExternalInput")
    wdbr_d = nc.dram_tensor("wdbr", [1, D], F32, kind="ExternalInput")
    wtbb_d = nc.dram_tensor("wtbb", [128, D], F32, kind="ExternalInput")
    mwt_d = nc.dram_tensor("mwt", [128, 8 * 41], F32, kind="ExternalInput")
    eye_d = nc.dram_tensor("eye32", [128, 32], BF, kind="ExternalInput")
    at_d = nc.dram_tensor("a_t", [16, 600], F32, kind="ExternalInput")
    bt_d = nc.dram_tensor("b_t", [16, 6144], F32, kind="ExternalInput")
    idb_d = nc.dram_tensor("idb", [128, 128], BF, kind="ExternalInput")
    idf_d = nc.dram_tensor("idf", [128, 128], F32, kind="ExternalInput")
    on_d = nc.dram_tensor("ones1", [1, 128], BF, kind="ExternalInput")
    xo_d = nc.dram_tensor("xo", [TT, D], F32, kind="ExternalOutput")
    cci_d = nc.dram_tensor("cc_in", [128, 16], F32)
    cco_d = nc.dram_tensor("cc_out", [128, 16], F32)

    es = contextlib.ExitStack()
    with es:
        sb = nc.sbuf_tensor
        psum = nc.psum_tensor
        ec = es.enter_context

        xbuf = [ec(sb(f"xbuf{i}", [128, D], F32)) for i in range(2)]
        qt = ec(sb("qt", [128, 8 * TT], BF))
        tfm = ec(sb("tfm", [128, 8 * 608], BF))
        val = ec(sb("val", [128, 5 * D], BF))
        wtbb = ec(sb("wtbb_s", [128, D], F32))
        mwt = ec(sb("mwt_s", [128, 8 * 41], F32))
        wp = ec(sb("wp_s", [128, 8 * 40], F32))
        b1t = ec(sb("b1t_s", [128, 8], F32))
        hm = ec(sb("hm_s", [128, 1], F32))
        s0 = ec(sb("s0_s", [128, 1], F32))
        s1 = ec(sb("s1_s", [128, 1], F32))
        sca = ec(sb("sca_s", [128, 1], F32))
        eye = ec(sb("eye_s", [128, 32], BF))
        idb = ec(sb("idb_s", [128, 128], BF))
        idf = ec(sb("idf_s", [128, 128], F32))
        ones = ec(sb("ones_s", [1, 128], BF))
        b2r = ec(sb("b2r_s", [1, D], F32))
        wdbrb = ec(sb("wdbr_b", [1, D], BF))
        part = ec(sb("part_s", [128, 8], F32))
        part4 = ec(sb("part4_s", [128, 32], F32))
        b1h = ec(sb("b1h_s", [128, 8], F32))
        ccs = ec(sb("ccs_s", [128, 16], F32))
        ccr = ec(sb("ccr_s", [128, 16], F32))
        tm1 = ec(sb("tm1_s", [128, 8], F32))
        tmine = ec(sb("tmine_s", [128, 8], F32))
        vsb = ec(sb("vsb_s", [128, 8], F32))
        vbf = ec(sb("vbf_s", [128, 8], BF))
        vrow = ec(sb("vrow_s", [1, D], F32))
        b2eb = ec(sb("b2e_b", [1, D], BF))
        epsc = ec(sb("eps_s", [128, 1], F32))
        stA = [ec(sb(f"stA{i}", [128, 16], F32)) for i in range(2)]

        S = Sems(nc, es, ["ld0", "ld1", "st0", "st1", "dwg", "dsm", "cc",
                          "sv", "ss", "sp", "sg"])
        blk = es.enter_context(nc.Block())

        @blk.sync
        def _(sync):
            V, SC, PE, G = nc.vector, nc.scalar, nc.tensor, nc.gpsimd
            SY = sync
            ld = ["ld0", "ld1"]
            st_ = ["st0", "st1"]

            for dst, src in [(b1t, b1t_d), (hm, hm_d), (s0, s0_d), (s1, s1_d),
                             (sca, sca_d), (eye, eye_d), (idb, idb_d), (idf, idf_d),
                             (ones, on_d), (b2r, b2r_d), (wtbb, wtbb_d),
                             (mwt, mwt_d)]:
                S.mark("dsm", SC.dma_start(out=dst[:, :], in_=src.ap()), 16)
            t_dsm = S.c["dsm"]

            t_eps = S.mark("sv", V.memset(epsc[:, :], EPS))

            t_xload = {}
            t_xfree = {}

            def load_xtile(idx, slot_idx, row0, halo=False):
                sl = ld[slot_idx]
                prev = t_xfree.get((slot_idx, "prev"))
                if prev is not None:
                    S.wait2(SY, prev)
                src = xh_d.ap() if halo else xc_d.ap()[row0:row0 + 128, :]
                t_xload[idx] = S.mark(sl, SY.dma_start(out=xbuf[slot_idx][:, :], in_=src), 16)

            t_qt_done = None
            with contextlib.ExitStack() as es_pa:
                p_sb = es_pa.enter_context(sb("p_sb", [128, 8 * W], BF))
                diagsA = es_pa.enter_context(sb("diagsA", [128, 2 * 5120], BF))
                t_diag = [0, 0]

                with contextlib.ExitStack() as es_a1:
                    lfm = es_a1.enter_context(sb("lfm", [128, 8 * W], BF))
                    lnb = [es_a1.enter_context(sb(f"lnb{i}", [128, D], BF)) for i in range(2)]
                    w1b = es_a1.enter_context(sb("w1b", [128, 8 * D], BF))
                    tpp = [es_a1.enter_context(psum(f"tpA{i}", [128, 128], BF)) for i in range(4)]
                    wps = [es_a1.enter_context(psum(f"wpsA{i}", [128, 512], F32)) for i in range(2)]

                    S.mark("dwg", G.dma_start(
                        out=w1b[:, :].rearrange("p (k d) -> p k d", k=8),
                        in_=w1_d.ap().rearrange("(k p) d -> p k d", p=128)), 16)
                    S.mark("dwg", G.dma_start(
                        out=tfm[:, :].rearrange("p (c e) -> p c e", c=8)[:, :, 600:606],
                        in_=wgt_d.ap().rearrange("(c p) e -> p c e", p=128)), 16)
                    S.mark("dwg", G.dma_start(out=wdbrb[:, :], in_=wdbr_d.ap()), 16)
                    t_dwg1 = S.c["dwg"]

                    t_tpev = {}
                    t_lnb_last = [0, 0]
                    t_lnbT = {}
                    t_lfm_ev = {}
                    S.wait(V, "dsm", t_dsm)
                    t_b1h = S.mark("sv", V.tensor_scalar(
                        b1h[:, :], b1t[:, :], hm[:, :], None, OP.mult))

                    t_wp = 0
                    for c in range(8):
                        t_wp = S.mark("sv", V.tensor_scalar(
                            wp[:, 40 * c:40 * (c + 1)], mwt[:, 41 * c + 1:41 * c + 41],
                            mwt[:, 41 * c:41 * c + 1], None, OP.subtract))

                    def diag_chunk(ci):
                        s_, j_ = divmod(ci, 4)
                        c_ = 4 * s_ + j_
                        base = 5120 * s_ + 1280 * j_
                        S.wait(V, "sv", t_wp)
                        tcc = S.mark("sv", V.tensor_copy(
                            diagsA[:, base:base + 1280].rearrange("p (k q) -> p k q", k=40),
                            wp[:, 40 * c_:40 * c_ + 40].unsqueeze(2)
                            .broadcast_to([128, 40, 32])))
                        S.wait(V, "sv", tcc)
                        t_diag[s_] = S.mark("sv", V.tensor_tensor(
                            diagsA[:, base:base + 1280].rearrange("p (r q) -> p r q", r=40),
                            diagsA[:, base:base + 1280].rearrange("p (r q) -> p r q", r=40),
                            eye[:, :].unsqueeze(1).broadcast_to([128, 40, 32]),
                            OP.mult))

                    load_xtile(0, 0, 0, halo=True)
                    load_xtile(1, 1, 0)

                    w1_pending = []

                    def w1_emit(cols0, n, is_halo, g, m):
                        pm = wps[m % 2]
                        prev = t_tpev.get(("w1", m % 2))
                        if prev is not None:
                            S.wait2(PE, prev)
                        S.wait(PE, "dwg", t_dwg1)
                        ii = None
                        for k in range(8):
                            ii = PE.matmul(
                                pm[:, 0:n],
                                w1b[:, 1024 * k + 128 * m: 1024 * k + 128 * (m + 1)],
                                lfm[:, W * k + cols0: W * k + cols0 + n],
                                start=(k == 0), stop=(k == 7))
                        t_mm = S.mark("sp", ii)
                        S.wait(SC, "sp", t_mm)
                        S.wait(SC, "dsm", t_dsm)
                        if is_halo:
                            S.wait(SC, "sv", t_b1h)
                            t_pe = S.mark("ss", SC.activation(
                                p_sb[:, W * m + cols0: W * m + cols0 + n],
                                pm[:, 0:n], AF.Identity,
                                scale=hm[:, :], bias=b1h[:, m:m + 1]))
                        else:
                            t_pe = S.mark("ss", SC.activation(
                                p_sb[:, W * m + cols0: W * m + cols0 + n],
                                pm[:, 0:n], AF.Identity,
                                bias=b1t[:, m:m + 1],
                                accum_out=part4[:, 4 * m + g:4 * m + g + 1]))
                        t_tpev[("w1", m % 2)] = ("ss", t_pe)

                    t_mv = {}
                    t_r = {}
                    t_ap = {}

                    def LNa(i):
                        pi = i % 2
                        xs = xbuf[pi]
                        stt = stA[pi]
                        S.wait(V, ld[pi], t_xload[i])
                        if i >= 2:
                            S.wait(V, "sv", t_lnb_last[pi])
                            S.wait(V, "ss", t_r[i - 2])
                        V.bn_stats(stt[:, 0:6], xs[:, 0:512])
                        t_s2 = S.mark("sv", V.bn_stats(stt[:, 6:12], xs[:, 512:1024]))
                        S.wait(V, "sv", t_s2)
                        t_mv[i] = S.mark("sv", V.bn_aggr(stt[:, 12:14], stt[:, 0:12]))

                    def LNb(i):
                        stt = stA[i % 2]
                        S.wait(SC, "sv", max(t_mv[i], t_eps))
                        t_l = S.mark("ss", SC.activation(stt[:, 14:15], stt[:, 13:14], AF.Ln, bias=epsc[:, :]))
                        S.wait(SC, "ss", t_l)
                        t_r[i] = S.mark("ss", SC.activation(stt[:, 14:15], stt[:, 14:15], AF.Exp, scale=-0.5))

                    def LNc(i):
                        pi = i % 2
                        stt = stA[pi]
                        S.wait(V, "ss", t_r[i])
                        if i >= 2:
                            S.wait(V, "sp", t_lnbT[i - 2])
                        t_ap[i] = S.mark("sv", V.tensor_scalar(
                            lnb[pi][:, :], xbuf[pi][:, :], stt[:, 12:13], stt[:, 14:15],
                            OP.subtract, OP.mult))
                        t_lnb_last[pi] = t_ap[i]
                        t_xfree[(pi, "prev")] = ("sv", t_ap[i])

                    def TR(i):
                        pi = i % 2
                        S.wait(PE, "sv", t_ap[i])
                        S.wait(PE, "dsm", t_dsm)
                        for c in range(8):
                            q = c % 4
                            prev = t_tpev.get(q)
                            if prev is not None:
                                S.wait2(PE, prev)
                            t_tp = S.mark("sp", PE.transpose(
                                tpp[q][:, :], lnb[pi][:, 128 * c:128 * (c + 1)], idb[:, :]))
                            dst = lfm[:, W * c + 128 * i: W * c + 128 * (i + 1)]
                            if c % 2 == 0:
                                S.wait(V, "sp", t_tp)
                                t_ev = ("sv", S.mark("sv", V.tensor_copy(dst, tpp[q][:, :])))
                            else:
                                S.wait(SC, "sp", t_tp)
                                t_ev = ("ss", S.mark("ss", SC.copy(dst, tpp[q][:, :])))
                            t_tpev[q] = t_ev
                            t_lfm_ev[(i, c)] = t_ev
                        t_lnbT[i] = S.c["sp"]

                    LNa(0)
                    LNb(0)
                    LNa(1)
                    for i in range(NT + 1):
                        LNc(i)
                        if i + 1 <= NT:
                            LNb(i + 1)
                        TR(i)
                        if 5 <= i <= 12:
                            diag_chunk(i - 5)
                        if i + 2 <= NT:
                            load_xtile(i + 2, (i + 2) % 2, 128 * (i + 1))
                            LNa(i + 2)
                        if i == 0:
                            S.wait2(PE, t_lfm_ev[(0, 6)])
                            S.wait2(PE, t_lfm_ev[(0, 7)])
                            w1_pending += [(0, 128, True, -1, m) for m in range(8)]
                        elif i >= 4 and i % 4 == 0:
                            S.wait2(PE, t_lfm_ev[(i, 6)])
                            S.wait2(PE, t_lfm_ev[(i, 7)])
                            w1_pending += [(128 + 512 * (i // 4 - 1), 512, False, i // 4 - 1, m)
                                           for m in range(8)]
                        budget = len(w1_pending) if i == NT else (4 if i == 0 else 2)
                        for _ in range(min(budget, len(w1_pending))):
                            w1_emit(*w1_pending.pop(0))

                    t_p_done_ss = S.c["ss"]
                    t_p_done_sv = S.c["sv"]
                    t_pe_a1 = S.c["sp"]

                    S.wait(V, "ss", t_p_done_ss)
                    t_part = S.mark("sv", V.tensor_reduce(
                        part[:, :], part4[:, :].rearrange("p (m g) -> p m g", m=8),
                        AX.X, OP.add))
                    S.wait(V, "sv", t_part)
                    V.tensor_scalar(ccs[:, 0:8], part[:, :], s0[:, :], None, OP.mult)
                    t_ccs = S.mark("sv", V.tensor_scalar(
                        ccs[:, 8:16], part[:, :], s1[:, :], None, OP.mult))

                w2b = ec(sb("w2b", [128, 8 * D], BF, side="right"))
                wdwtb = ec(sb("wdwtb", [128, 8 * D], BF, side="right"))
                with contextlib.ExitStack() as es_a2:
                    wtwtb = es_a2.enter_context(sb("wtwtb", [128, 8 * D], BF))
                    atb = es_a2.enter_context(sb("a_tb", [16, 600], BF))
                    btb = es_a2.enter_context(sb("b_tb", [16, 6144], BF))

                    S.wait(G, "sp", t_pe_a1)
                    for dst, srcd in [(w2b, w2p_d), (wdwtb, wdwt_d), (wtwtb, wtwt_d)]:
                        S.mark("dwg", G.dma_start(
                            out=dst[:, :].rearrange("p (k d) -> p k d", k=8),
                            in_=srcd.ap().rearrange("(k p) d -> p k d", p=128)), 16)
                    S.mark("dwg", G.dma_start(out=atb[:, :], in_=at_d.ap()), 16)
                    S.mark("dwg", G.dma_start(out=btb[:, :], in_=bt_d.ap()), 16)
                    S.wait(G, "sv", t_ccs)
                    S.mark("dwg", G.dma_start(out=cci_d.ap(), in_=ccs[:, :]), 16)
                    t_dwg2 = S.c["dwg"]
                    S.wait(G, "dwg", S.c["dwg"])
                    t_cc = S.mark("cc", G.collective_compute(
                        "AllReduce", OP.add, replica_groups=[list(range(8))],
                        ins=[cci_d.ap()], outs=[cco_d.ap()]))
                    S.wait(G, "cc", t_cc)
                    S.mark("dwg", G.dma_start(out=ccr[:, :], in_=cco_d.ap()), 16)
                    t_dwg_cc = S.c["dwg"]

                    with contextlib.ExitStack() as es_cv:
                        cv = [es_cv.enter_context(psum(f"cv{i}", [128, 512], F32)) for i in range(8)]
                        t_cv_ev = {}
                        S.wait(PE, "ss", t_p_done_ss)
                        S.wait(PE, "sv", t_p_done_sv)
                        for s in range(2):
                            S.wait(PE, "sv", t_diag[s])
                            for tg in range(4):
                                pset = (s * 4 + tg) % 2
                                for i4 in range(4):
                                    prev = t_cv_ev.get((pset, i4))
                                    if prev is not None:
                                        S.wait2(PE, prev)
                                ii = None
                                for k in range(TAPS):
                                    for j4 in range(4):
                                        for i4 in range(4):
                                            base = 5120 * s + (40 * j4 + k) * 32
                                            mv0 = W * (4 * s + j4) + 128 + 512 * tg - k
                                            ii = PE.matmul(
                                                cv[4 * pset + i4][32 * j4:32 * (j4 + 1), :],
                                                diagsA[32 * i4:32 * (i4 + 1), base:base + 32],
                                                p_sb[32 * i4:32 * (i4 + 1), mv0:mv0 + 512],
                                                start=(k == 0), stop=(k == TAPS - 1),
                                                tile_position=(32 * i4, 32 * j4),
                                                skip_group_check=True)
                                t_mm = S.mark("sp", ii)
                                for i4 in range(4):
                                    dst = qt[:, TT * (4 * s + i4) + 512 * tg:
                                             TT * (4 * s + i4) + 512 * (tg + 1)]
                                    if i4 % 2 == 0:
                                        S.wait(V, "sp", t_mm)
                                        t_cv_ev[(pset, i4)] = ("sv", S.mark(
                                            "sv", V.tensor_copy(dst, cv[4 * pset + i4][:, :])))
                                    else:
                                        S.wait(SC, "sp", t_mm)
                                        t_cv_ev[(pset, i4)] = ("ss", S.mark(
                                            "ss", SC.copy(dst, cv[4 * pset + i4][:, :])))
                        t_qt_done = S.c["sv"]
                        t_qt_done_ss = S.c["ss"]
                    with contextlib.ExitStack() as es_b1:
                        bld = [es_b1.enter_context(psum(f"bld{i}", [128, 1024], F32)) for i in range(2)]
                        S.wait(PE, "sv", t_qt_done)
                        S.wait(PE, "ss", t_qt_done_ss)
                        S.wait(PE, "dwg", t_dwg2)
                        t_tfm_ev = [0] * 8
                        for c in range(8):
                            pm = bld[c % 2]
                            if c >= 2:
                                S.wait(PE, "sv", t_tfm_ev[c - 2])
                            ii = None
                            for e in range(6):
                                ii = PE.matmul(pm[:, 128 * e:128 * e + 100],
                                               btb[:, 1024 * e + 128 * c: 1024 * e + 128 * (c + 1)],
                                               atb[:, 100 * e:100 * (e + 1)],
                                               start=True, stop=True)
                            t_mm = S.mark("sp", ii)
                            S.wait(V, "sp", t_mm)
                            t_tfm_ev[c] = S.mark("sv", V.tensor_copy(
                                tfm[:, 608 * c: 608 * c + 600].rearrange("p (e l) -> p e l", e=6),
                                pm[:, :].rearrange("p (e l) -> p e l", e=8)[:, 0:6, 0:100]))
                        t_tfm_done = t_tfm_ev[7]

                        S.wait(PE, "dwg", t_dwg2)
                        t_zv = S.mark("sv", V.memset(val[64:128, D * 4:D * 5], 0.0))
                        t_val_ev = [0] * 10
                        vi = 0
                        for v in range(5):
                            mrows = 128 if v < 4 else 88
                            for jh in range(2):
                                pm = bld[vi % 2]
                                if vi >= 2:
                                    S.wait(PE, "sv", t_val_ev[vi - 2])
                                S.wait(PE, "sv", t_tfm_done)
                                ii = None
                                for k in range(8):
                                    ii = PE.matmul(
                                        pm[0:mrows, 0:512],
                                        tfm[:, 608 * k + 128 * v: 608 * k + 128 * v + mrows],
                                        wtwtb[:, 1024 * k + 512 * jh: 1024 * k + 512 * (jh + 1)],
                                        start=(k == 0), stop=(k == 7))
                                t_mm = S.mark("sp", ii)
                                S.wait(V, "sp", t_mm)
                                S.wait(V, "dsm", t_dsm)
                                S.wait(V, "sv", t_zv)
                                t_val_ev[vi] = S.mark("sv", V.tensor_tensor(
                                    val[0:mrows, D * v + 512 * jh: D * v + 512 * (jh + 1)],
                                    pm[0:mrows, 0:512], wtbb[0:mrows, 512 * jh:512 * (jh + 1)],
                                    OP.add))
                                vi += 1
                        t_val_done = max(t_val_ev[9], t_zv)

                        S.wait(V, "dwg", t_dwg_cc)
                        t_t1 = S.mark("sv", V.tensor_scalar(
                            tm1[:, :], ccr[:, 8:16], s1[:, :], None, OP.mult))
                        S.wait(V, "sv", t_t1)
                        t_tm = S.mark("sv", V.scalar_tensor_tensor(
                            tmine[:, :], ccr[:, 0:8], s0[:, :], tm1[:, :], OP.mult, OP.add))
                        S.wait(V, "sv", t_tm)
                        t_v = S.mark("sv", V.tensor_tensor(
                            vsb[:, :], tmine[:, :],
                            mwt[:, :].rearrange("p (c k) -> p c k", c=8)[:, :, 0:1]
                            .rearrange("p c k -> p (c k)"),
                            OP.mult))
                        S.wait(V, "sv", t_v)
                        t_vb = S.mark("sv", V.tensor_copy(vbf[:, :], vsb[:, :]))
                        S.wait(PE, "sv", max(t_vb, t_val_ev[9]))
                        S.wait(PE, "dwg", t_dwg2)
                        t_vre = [0, 0]
                        for jh in range(2):
                            pm = bld[jh]
                            ii = None
                            for k in range(8):
                                ii = PE.matmul(pm[0:1, 0:512], vbf[:, k:k + 1],
                                               w2b[:, 1024 * k + 512 * jh: 1024 * k + 512 * (jh + 1)],
                                               start=(k == 0), stop=(k == 7))
                            t_mm = S.mark("sp", ii)
                            S.wait(V, "sp", t_mm)
                            t_vre[jh] = S.mark("sv", V.tensor_copy(
                                vrow[0:1, 512 * jh:512 * (jh + 1)], pm[0:1, 0:512]))
                        S.wait(V, "sv", max(t_vre))
                        S.wait(V, "dsm", t_dsm)
                        t_vr = S.mark("sv", V.tensor_tensor(
                            vrow[0:1, :], vrow[0:1, :], b2r[0:1, :], OP.add))
                        S.wait(V, "sv", t_vr)
                        t_b2e = S.mark("sv", V.tensor_copy(b2eb[0:1, :], vrow[0:1, :]))


            with contextlib.ExitStack() as es_b:
                x2fm = [es_b.enter_context(sb(f"x2fm{i}", [128, D], BF)) for i in range(2)]
                xmid = [es_b.enter_context(sb(f"xmid{i}", [128, D], F32)) for i in range(2)]
                x2f = [es_b.enter_context(sb(f"x2f{i}", [128, D], F32)) for i in range(2)]
                o2s = [es_b.enter_context(sb(f"o2s{i}", [128, D], F32)) for i in range(2)]
                m600 = [es_b.enter_context(sb(f"m600_{i}", [128, 640], F32)) for i in range(2)]
                m600t = [es_b.enter_context(sb(f"m600t{i}", [128, 640], BF)) for i in range(2)]
                asel = [es_b.enter_context(sb(f"asel{i}", [128, 100], F32)) for i in range(2)]
                aexp = [es_b.enter_context(sb(f"aexp{i}", [128, 100], F32)) for i in range(2)]
                dffm = [es_b.enter_context(sb(f"dffm{i}", [128, D], BF)) for i in range(2)]
                fout = [es_b.enter_context(sb(f"fout{i}", [128, D], F32)) for i in range(2)]
                xos = [es_b.enter_context(sb(f"xos{i}", [128, D], F32)) for i in range(2)]
                stB = [es_b.enter_context(sb(f"stB{i}", [128, 16], F32)) for i in range(2)]
                st3 = [es_b.enter_context(sb(f"st3_{i}", [128, 16], F32)) for i in range(2)]
                gB = [es_b.enter_context(sb(f"gB{i}", [128, 24], F32)) for i in range(2)]
                gs6 = [es_b.enter_context(sb(f"gs6_{i}", [128, 6], F32)) for i in range(2)]
                pAD = [es_b.enter_context(psum(f"pAD{i}", [128, 1024], F32)) for i in range(2)]
                pSF = [es_b.enter_context(psum(f"pSF{i}", [128, 1024], F32)) for i in range(2)]

                S.wait(V, "sv", S.c["sv"])
                S.wait(V, "ss", S.c["ss"])
                S.wait(V, "sp", S.c["sp"])
                t_ms = [S.mark("sv", V.memset(m600[i][:, :], 0.0)) for i in range(2)]
                t_store_cnt = [0, 0]
                TE = {}
                tok = {}

                load_xtile(("b", 0), 0, 0)
                load_xtile(("b", 1), 1, 128)

                def ln_chain(stt, src_ap, nmr=True):
                    V.bn_stats(stt[:, 0:6], src_ap[:, 0:512])
                    t_s2 = S.mark("sv", V.bn_stats(stt[:, 6:12], src_ap[:, 512:1024]))
                    S.wait(V, "sv", t_s2)
                    t_mv = S.mark("sv", V.bn_aggr(stt[:, 12:14], stt[:, 0:12]))
                    S.wait(SC, "sv", max(t_mv, t_eps))
                    t_l = S.mark("ss", SC.activation(stt[:, 14:15], stt[:, 13:14], AF.Ln, bias=epsc[:, :]))
                    S.wait(SC, "ss", t_l)
                    t_r = S.mark("ss", SC.activation(stt[:, 14:15], stt[:, 14:15], AF.Exp, scale=-0.5))
                    t_nmr = None
                    if nmr:
                        S.wait(V, "ss", t_r)
                        t_nmr = S.mark("sv", V.tensor_scalar(
                            stt[:, 15:16], stt[:, 12:13], stt[:, 14:15], -1.0, OP.mult, OP.mult))
                    return t_r, t_nmr

                def fPE(tb):
                    pi = tb % 2
                    if tb >= 2:
                        for eng in (V, SC, PE, G):
                            for snm in ("sv", "ss", "sp", "sg"):
                                S.wait(eng, snm, TE[tb - 2][snm])
                    if tb >= 1:
                        for pr in tok[(tb - 1, "x2Tev_AD")]:
                            S.wait2(PE, pr)
                    if tb == 0:
                        S.wait(PE, "sv", max(t_qt_done, t_b2e))
                        S.wait(PE, "ss", t_qt_done_ss)
                        S.wait(PE, "dwg", t_dwg2)
                    ii = None
                    for jh in range(2):
                        for k in range(8):
                            ii = PE.matmul(
                                pAD[pi][:, 512 * jh:512 * (jh + 1)],
                                qt[:, TT * k + 128 * tb: TT * k + 128 * (tb + 1)],
                                w2b[:, 1024 * k + 512 * jh: 1024 * k + 512 * (jh + 1)],
                                start=(k == 0), stop=False)
                        ii = PE.matmul(pAD[pi][:, 512 * jh:512 * (jh + 1)],
                                       ones[0:1, :], b2eb[0:1, 512 * jh:512 * (jh + 1)],
                                       start=False, stop=True)
                    tok[(tb, "out2")] = S.mark("sp", ii)

                def fV(tb):
                    pi = tb % 2
                    stt = stB[pi]
                    S.wait(V, "sp", tok[(tb, "out2")])
                    _, t_nmr2 = ln_chain(stt, pAD[pi])
                    S.wait(SC, "sp", tok[(tb, "out2")])
                    S.wait(SC, "sv", t_nmr2)
                    t_t2 = S.mark("ss", SC.activation(
                        o2s[pi][:, :], pAD[pi][:, :], AF.Identity,
                        scale=stt[:, 14:15], bias=stt[:, 15:16]))
                    tok[(tb, "pAfree")] = ("ss", t_t2)
                    S.wait(V, "ss", t_t2)
                    S.wait(V, ld[pi], t_xload[("b", tb)])
                    t_xmid = S.mark("sv", V.tensor_tensor(
                        xmid[pi][:, :], o2s[pi][:, :], xbuf[pi][:, :], OP.add))
                    t_xfree[(pi, "prev")] = ("sv", t_xmid)
                    tok[(tb, "xmid")] = t_xmid
                    if tb + 2 < NT:
                        load_xtile(("b", tb + 2), (tb + 2) % 2, 128 * (tb + 2))
                    S.wait(V, "sv", t_xmid)
                    t_r3, _ = ln_chain(st3[pi], xmid[pi], nmr=False)
                    S.wait(V, "ss", t_r3)
                    t_x2 = S.mark("sv", V.tensor_scalar(
                        x2f[pi][:, :], xmid[pi][:, :], st3[pi][:, 12:13], st3[pi][:, 14:15],
                        OP.subtract, OP.mult))
                    tok[(tb, "x2")] = t_x2

                def midT(tb):
                    pi = tb % 2
                    S.wait(PE, "sv", tok[(tb, "x2")])
                    if tb >= 1:
                        S.wait2(PE, tok[(tb - 1, "pDfree")])
                    if tb >= 1:
                        for pr in tok[(tb - 1, "m6tev")]:
                            S.wait2(PE, pr)
                    tpev = [None, None, None, None]
                    for c in range(8):
                        q = c % 4
                        slot = (pSF[pi] if q < 2 else pAD[1 - pi])
                        col = 512 * (q % 2)
                        if tpev[q] is not None:
                            S.wait2(PE, tpev[q])
                        t_tp = S.mark("sp", PE.transpose(
                            slot[:, col:col + 128], x2f[pi][:, 128 * c:128 * (c + 1)], idf[:, :]))
                        dst = x2fm[pi][:, 128 * c:128 * (c + 1)]
                        if c % 2 == 0:
                            S.wait(SC, "sp", t_tp)
                            tpev[q] = ("ss", S.mark("ss", SC.copy(dst, slot[:, col:col + 128])))
                        else:
                            S.wait(V, "sp", t_tp)
                            tpev[q] = ("sv", S.mark("sv", V.tensor_copy(dst, slot[:, col:col + 128])))
                    tok[(tb, "x2Tev_AD")] = [tpev[2], tpev[3]]
                    tok[(tb, "x2fm_v")] = S.c["sv"]
                    tok[(tb, "x2fm_s")] = S.c["ss"]

                def midR(tb):
                    pi = tb % 2
                    g = gB[pi]
                    S.wait(PE, "sv", max(tok[(tb, "x2fm_v")], t_tfm_done))
                    S.wait(PE, "ss", tok[(tb, "x2fm_s")])
                    ii = None
                    for k in range(8):
                        ii = PE.matmul(pSF[pi][:, 512:606],
                                       x2fm[pi][:, 128 * k:128 * (k + 1)],
                                       tfm[:, 608 * k + 512: 608 * k + 606],
                                       start=(k == 0), stop=(k == 7))
                    t_sc2 = S.mark("sp", ii)
                    for k in range(8):
                        ii = PE.matmul(pSF[pi][:, 0:512],
                                       x2fm[pi][:, 128 * k:128 * (k + 1)],
                                       tfm[:, 608 * k: 608 * k + 512],
                                       start=(k == 0), stop=(k == 7))
                    t_sc = S.mark("sp", ii)

                    S.wait2(PE, tok[(tb, "pAfree")])
                    for jh in range(2):
                        for k in range(8):
                            PE.matmul(
                                pAD[pi][:, 512 * jh:512 * (jh + 1)],
                                x2fm[pi][:, 128 * k:128 * (k + 1)],
                                wdwtb[:, 1024 * k + 512 * jh: 1024 * k + 512 * (jh + 1)],
                                start=(k == 0), stop=False, skip_group_check=True)

                    S.wait(V, "sp", t_sc2)
                    t_m6 = S.mark("sv", V.tensor_reduce(
                        g[:, 0:1], pSF[pi][:, 600:606], AX.X, OP.max))
                    S.wait(V, "sv", t_m6)
                    t_hits = S.mark("sv", V.tensor_scalar(
                        g[:, 1:7], pSF[pi][:, 600:606], g[:, 0:1], None, OP.is_ge))
                    S.wait(V, "sv", t_hits)
                    t_cum = S.mark("sv", V.tensor_tensor_scan(
                        g[:, 7:13], g[:, 1:7], g[:, 1:7], 0.0, OP.add, OP.bypass))
                    S.wait(V, "sv", t_cum)
                    t_g6 = S.mark("sv", V.scalar_tensor_tensor(
                        g[:, 13:19], g[:, 7:13], 1.0, g[:, 1:7], OP.is_equal, OP.mult))
                    S.wait(V, "sv", t_g6)
                    S.wait(V, "sp", t_sc)
                    t_a0 = S.mark("sv", V.tensor_scalar(
                        asel[pi][:, :], pSF[pi][:, 0:100], g[:, 13:14], None, OP.mult))
                    t_a1 = S.mark("sv", V.tensor_scalar(
                        aexp[pi][:, :], pSF[pi][:, 100:200], g[:, 14:15], None, OP.mult))
                    for e in range(2, 6):
                        if e % 2 == 0:
                            S.wait(V, "sv", t_a0)
                            t_a0 = S.mark("sv", V.scalar_tensor_tensor(
                                asel[pi][:, :], pSF[pi][:, 100 * e:100 * (e + 1)], g[:, 13 + e:14 + e],
                                asel[pi][:, :], OP.mult, OP.add))
                        else:
                            S.wait(V, "sv", t_a1)
                            t_a1 = S.mark("sv", V.scalar_tensor_tensor(
                                aexp[pi][:, :], pSF[pi][:, 100 * e:100 * (e + 1)], g[:, 13 + e:14 + e],
                                aexp[pi][:, :], OP.mult, OP.add))
                    S.wait(V, "sv", max(t_a0, t_a1))
                    t_as = S.mark("sv", V.tensor_tensor(
                        asel[pi][:, :], asel[pi][:, :], aexp[pi][:, :], OP.add))
                    tok[(tb, "gather")] = t_as
                    S.wait(V, "sv", t_as)
                    t_am = S.mark("sv", V.tensor_reduce(g[:, 19:20], asel[pi][:, :], AX.X, OP.max))
                    S.wait(V, "sv", t_am)
                    t_nbm = S.mark("sv", V.tensor_scalar(
                        g[:, 20:21], g[:, 19:20], -BETA, None, OP.mult))
                    S.wait(SC, "sv", max(t_nbm, t_as))
                    t_ae = S.mark("ss", SC.activation(
                        aexp[pi][:, :], asel[pi][:, :], AF.Exp,
                        bias=g[:, 20:21], scale=BETA, accum_out=g[:, 21:22]))
                    S.wait(V, "ss", t_ae)
                    t_rs = S.mark("sv", V.reciprocal(g[:, 22:23], g[:, 21:22]))
                    S.wait(V, "sv", max(t_rs, t_ms[pi]))
                    t_grs = S.mark("sv", V.tensor_scalar(
                        gs6[pi][:, :], g[:, 13:19], g[:, 22:23], None, OP.mult))
                    S.wait(SC, "sv", t_grs)
                    S.wait(V, "sv", t_grs)
                    S.wait(SC, "ss", t_ae)
                    sct = []
                    for e in range(6):
                        if e % 2 == 0:
                            sct.append(("sv", S.mark("sv", V.tensor_scalar(
                                m600[pi][:, 100 * e + 1:100 * (e + 1)],
                                aexp[pi][:, 1:100], gs6[pi][:, e:e + 1], None, OP.mult))))
                        else:
                            sct.append(("ss", S.mark("ss", SC.activation(
                                m600[pi][:, 100 * e + 1:100 * (e + 1)],
                                aexp[pi][:, 1:100], AF.Identity, scale=gs6[pi][:, e:e + 1]))))
                    tok[(tb, "scatter")] = sct

                def backP1(tb):
                    pi = tb % 2
                    S.wait(PE, "sv", tok[(tb, "gather")])
                    if tb >= 1:
                        S.wait2(PE, tok[(tb - 1, "dffm")])
                    sct = tok[(tb, "scatter")]
                    tpev = [None, None, None, None]
                    for c in range(5):
                        q = c % 4
                        slot = (pSF[pi] if q < 2 else pSF[1 - pi])
                        col = 512 * (q % 2)
                        if tpev[q] is not None:
                            S.wait2(PE, tpev[q])
                        mx = {}
                        for e in range(6):
                            if 100 * e + 1 < 128 * (c + 1) and 100 * (e + 1) > 128 * c:
                                mx[sct[e][0]] = sct[e]
                        for pr in mx.values():
                            S.wait2(PE, pr)
                        t_tp = S.mark("sp", PE.transpose(
                            slot[:, col:col + 128], m600[pi][:, 128 * c:128 * (c + 1)], idf[:, :]))
                        dst = m600t[pi][:, 128 * c:128 * (c + 1)]
                        if c % 2 == 0:
                            S.wait(SC, "sp", t_tp)
                            tpev[q] = ("ss", S.mark("ss", SC.copy(dst, slot[:, col:col + 128])))
                        else:
                            S.wait(V, "sp", t_tp)
                            tpev[q] = ("sv", S.mark("sv", V.tensor_copy(dst, slot[:, col:col + 128])))
                    tok[(tb, "m6tev")] = [("ss", S.c["ss"]), ("sv", S.c["sv"])]
                    t_m6t_v = S.c["sv"]
                    t_m6t_s = S.c["ss"]

                    S.wait(PE, "sv", max(t_m6t_v, t_val_done))
                    S.wait(PE, "ss", t_m6t_s)
                    ii = None
                    for m in range(8):
                        for kel in range(5):
                            ii = PE.matmul(
                                pSF[pi][:, 128 * m:128 * (m + 1)],
                                val[:, D * kel + 128 * m: D * kel + 128 * (m + 1)],
                                m600t[pi][:, 128 * kel:128 * (kel + 1)],
                                start=(kel == 0), stop=(kel == 4),
                                skip_group_check=True)
                    t_df = S.mark("sp", ii)
                    S.wait(SC, "sp", t_df)
                    t_dfb = S.mark("ss", SC.copy(dffm[pi][:, :], pSF[pi][:, :]))
                    tok[(tb, "dffm")] = ("ss", t_dfb)

                def backP2(tb):
                    pi = tb % 2
                    stt = stB[pi]
                    S.wait2(PE, tok[(tb, "dffm")])
                    S.wait(PE, "dwg", t_dwg_cc)
                    ii = None
                    for jh in range(2):
                        for k in range(8):
                            ii = PE.matmul(
                                pAD[pi][:, 512 * jh:512 * (jh + 1)],
                                dffm[pi][:, 128 * k:128 * (k + 1)],
                                wdwtb[:, 1024 * k + 512 * jh: 1024 * k + 512 * (jh + 1)],
                                start=False, stop=False, skip_group_check=True)
                        ii = PE.matmul(pAD[pi][:, 512 * jh:512 * (jh + 1)],
                                       ones[0:1, :], wdbrb[0:1, 512 * jh:512 * (jh + 1)],
                                       start=False, stop=True, skip_group_check=True)
                    t_dl = S.mark("sp", ii)

                    S.wait(SC, "sp", t_dl)
                    if t_store_cnt[pi] > 0:
                        S.wait(SC, st_[pi], 16 * t_store_cnt[pi])
                    t_t5 = S.mark("ss", SC.activation(
                        xos[pi][:, :], pAD[pi][:, :], AF.Identity, scale=sca[:, :]))
                    tok[(tb, "pDfree")] = ("ss", t_t5)
                    S.wait(V, "ss", t_t5)
                    S.wait(V, "sv", tok[(tb, "x2")])
                    t_fo = S.mark("sv", V.tensor_tensor(
                        fout[pi][:, :], xos[pi][:, :], x2f[pi][:, :], OP.add))
                    tok[(tb, "fout")] = t_fo

                def backP3(tb):
                    pi = tb % 2
                    stt = stB[pi]
                    S.wait(V, "sv", tok[(tb, "fout")])
                    _, t_nmr4 = ln_chain(stt, fout[pi])
                    S.wait(SC, "sv", t_nmr4)
                    t_t4 = S.mark("ss", SC.activation(
                        o2s[pi][:, :], fout[pi][:, :], AF.Identity,
                        scale=stt[:, 14:15], bias=stt[:, 15:16]))
                    S.wait(G, "ss", t_t4)
                    S.wait(G, "sv", tok[(tb, "xmid")])
                    t_xo = S.mark("sg", G.tensor_tensor(
                        xos[pi][:, :], o2s[pi][:, :], xmid[pi][:, :], OP.add))
                    S.wait(SY, "sg", t_xo)
                    S.mark(st_[pi], SY.dma_start(
                        out=xo_d.ap()[128 * tb:128 * (tb + 1), :], in_=xos[pi][:, :]), 16)
                    t_store_cnt[pi] += 1
                    TE[tb] = {snm: S.c[snm] for snm in ("sv", "ss", "sp", "sg")}

                fPE(0)
                fV(0)
                midT(0)
                midR(0)
                for tb in range(NT):
                    if tb + 1 < NT:
                        fPE(tb + 1)
                    backP1(tb)
                    if tb + 1 < NT:
                        fV(tb + 1)
                    backP2(tb)
                    if tb + 1 < NT:
                        midT(tb + 1)
                    backP3(tb)
                    if tb + 1 < NT:
                        midR(tb + 1)

                S.wait(SY, "st0", 16 * t_store_cnt[0])
                S.wait(SY, "st1", 16 * t_store_cnt[1])
    return nc


_NC_CACHE = {}


def _get_nc():
    if "nc" not in _NC_CACHE:
        _NC_CACHE["nc"] = build_nc()
    return _NC_CACHE["nc"]


def kernel(L0_attn_input_data, L0_attn_W1, L0_attn_bias1, L0_attn_W2,
           L0_attn_bias2, L0_attn_memory_weights, L0_ffn_A, L0_ffn_B,
           L0_ffn_w_gate, L0_ffn_wt_w, L0_ffn_wt_b, L0_ffn_wd_w, L0_ffn_wd_b,
           L0_ffn_scale):
    x = np.ascontiguousarray(np.asarray(L0_attn_input_data, dtype=np.float32))
    W1 = np.ascontiguousarray(np.asarray(L0_attn_W1, dtype=np.float32))
    b1 = np.asarray(L0_attn_bias1, dtype=np.float32)
    W2 = np.ascontiguousarray(np.asarray(L0_attn_W2, dtype=np.float32))
    b2 = np.asarray(L0_attn_bias2, dtype=np.float32)
    mw = np.ascontiguousarray(np.asarray(L0_attn_memory_weights, dtype=np.float32))
    A = np.ascontiguousarray(np.asarray(L0_ffn_A, dtype=np.float32))
    Bm = np.ascontiguousarray(np.asarray(L0_ffn_B, dtype=np.float32))
    wg = np.ascontiguousarray(np.asarray(L0_ffn_w_gate, dtype=np.float32))
    wtw = np.ascontiguousarray(np.asarray(L0_ffn_wt_w, dtype=np.float32))
    wtb = np.asarray(L0_ffn_wt_b, dtype=np.float32)
    wdw = np.ascontiguousarray(np.asarray(L0_ffn_wd_w, dtype=np.float32))
    wdb = np.asarray(L0_ffn_wd_b, dtype=np.float32)
    sc = np.asarray(L0_ffn_scale, dtype=np.float32)

    perm = []
    for s in range(2):
        for i in range(4):
            for j in range(4):
                base = 32 * (16 * s + 4 * j + i)
                perm.extend(range(base, base + 32))
    w2p = np.ascontiguousarray(W2[np.asarray(perm), :])

    shared = {
        "w1": W1,
        "w2p": w2p,
        "wtwt": np.ascontiguousarray(wtw.T),
        "wdwt": np.ascontiguousarray(wdw.T),
        "wgt": wg,
        "b1t": np.ascontiguousarray(b1.reshape(8, 128).T),
        "b2r": b2.reshape(1, D),
        "wdbr": wdb.reshape(1, D),
        "wtbb": np.ascontiguousarray(np.tile(wtb.reshape(1, D), (128, 1))),
        "mwt": np.ascontiguousarray(
            mw.T.reshape(8, 128, 41).transpose(1, 0, 2).reshape(128, 8 * 41)),
        "eye32": np.eye(32, dtype=BFNP)[np.tile(np.arange(32), 4)].reshape(128, 32),
        "a_t": np.ascontiguousarray(A.transpose(2, 0, 1).reshape(16, 600)),
        "b_t": np.ascontiguousarray(Bm.transpose(1, 0, 2).reshape(16, 6144)),
        "idb": np.eye(128, dtype=BFNP),
        "idf": np.eye(128, dtype=np.float32),
        "ones1": np.ones((1, 128), dtype=BFNP),
        "sca": np.tile(sc.reshape(1, 1), (128, 1)).astype(np.float32),
    }

    in_maps = []
    for core in range(8):
        b = core // 4
        j = core % 4
        t0 = 2048 * j
        m = {"xc": np.ascontiguousarray(x[b, t0:t0 + 2048])}
        if j == 0:
            m["xh"] = np.zeros((128, D), dtype=np.float32)
            m["hm"] = np.zeros((128, 1), dtype=np.float32)
        else:
            m["xh"] = np.ascontiguousarray(x[b, t0 - 128:t0])
            m["hm"] = np.ones((128, 1), dtype=np.float32)
        m["s0"] = np.full((128, 1), 1.0 if b == 0 else 0.0, dtype=np.float32)
        m["s1"] = np.full((128, 1), 1.0 if b == 1 else 0.0, dtype=np.float32)
        m.update(shared)
        in_maps.append(m)

    nc = _get_nc()
    res = run_bass_kernel_spmd(nc, in_maps, core_ids=list(range(8)))
    out = np.empty((2, 8192, D), dtype=np.float32)
    for core in range(8):
        b = core // 4
        j = core % 4
        out[b, 2048 * j:2048 * (j + 1)] = res.results[core]["xo"]
    return out
